# version 03
# baseline (speedup 1.0000x reference)
import numpy as np
import concourse.bass as bass
import concourse.mybir as mybir
from concourse.bass_utils import run_bass_kernel_spmd
from concourse.alu_op_type import AluOpType as ALU

F32 = mybir.dt.float32
AF = mybir.ActivationFunctionType
AX = mybir.AxisListType

ENGINES = ("sync", "tensor", "vector", "scalar", "gpsimd")
DMA_ENGINES = ("sync", "scalar", "gpsimd")



class _Op:
    __slots__ = ("eng", "fn", "deps", "inc", "seq", "dma", "dma_idx", "dkey",
                 "idx", "tag", "bar_waits")

    def __init__(self, eng, fn, dma, idx, tag):
        self.eng, self.fn, self.dma, self.idx, self.tag = eng, fn, dma, idx, tag
        self.deps = set()
        self.inc = False
        self.seq = 0
        self.dma_idx = -1
        self.dkey = None
        self.bar_waits = None


class Prog:
    def __init__(self):
        self.ops = []
        self.state = {}
        self.dma_count = {}

    def _st(self, key):
        return self.state.setdefault(key, {"w": [], "r": []})

    @staticmethod
    def _bank(key, lo, hi):
        if key.startswith("ps"):
            return (lo // 512) * 512, ((hi + 511) // 512) * 512
        return lo, hi

    def op(self, eng, fn, reads=(), writes=(), dma=False, tag=""):
        o = _Op(eng, fn, dma, len(self.ops), tag)
        reads = [(k,) + self._bank(k, lo, hi) for (k, lo, hi) in reads]
        writes = [(k,) + self._bank(k, lo, hi) for (k, lo, hi) in writes]
        deps = set()
        for (k, lo, hi) in reads:
            st = self._st(k)
            for (l, h, w) in st["w"]:
                if l < hi and lo < h:
                    deps.add(w)
        for (k, lo, hi) in writes:
            st = self._st(k)
            for (l, h, w) in st["w"]:
                if l < hi and lo < h:
                    deps.add(w)
            for (l, h, r) in st["r"]:
                if l < hi and lo < h:
                    deps.add(r)
        deps.discard(o)
        for (k, lo, hi) in reads:
            self._st(k)["r"].append((lo, hi, o))
        for (k, lo, hi) in writes:
            st = self._st(k)
            st["w"] = [e_ for e_ in st["w"] if not (lo <= e_[0] and e_[1] <= hi)]
            st["r"] = [e_ for e_ in st["r"] if not (lo <= e_[0] and e_[1] <= hi)]
            st["w"].append((lo, hi, o))
        o.deps = deps
        if dma:
            slot = writes[0][0] if writes else "anon"
            if reads and tag.endswith(("_st", "_stq", "_stv")):
                slot = reads[0][0]
            o.dkey = (eng, slot)
            o.dma_idx = self.dma_count.get(o.dkey, 0)
            self.dma_count[o.dkey] = o.dma_idx + 1
        self.ops.append(o)
        return o

    def barrier(self):
        snap_d = dict(self.dma_count)
        for e in ENGINES:
            o = _Op(e, lambda eng: eng.nop(), False, len(self.ops), "barrier")
            o.bar_waits = snap_d
            self.ops.append(o)
        self.state = {}
        self._bar_marks = getattr(self, "_bar_marks", [])

    def finalize(self):
        for o in self.ops:
            for d in o.deps:
                if d.dma:
                    continue
                if d.eng == "tensor" and o.eng == "tensor":
                    continue
                d.inc = True
        lastop = {}
        for o in self.ops:
            if o.bar_waits is not None:
                for e2, lo in lastop.items():
                    if not lo.dma:
                        lo.inc = True
                    o.deps.add(lo)
            else:
                lastop[o.eng] = o
        seq = {e: 0 for e in ENGINES}
        for o in self.ops:
            if o.inc and not o.dma:
                seq[o.eng] += 1
            o.seq = seq[o.eng]
        per_eng = {e: [] for e in ENGINES}
        for o in self.ops:
            per_eng[o.eng].append(o)
        self.per_eng = per_eng

    def emit(self, eng_name, eng, psems, dsems):
        water = {}
        for o in self.per_eng[eng_name]:
            waits = {}
            for d in o.deps:
                if d.eng == "tensor" and o.eng == "tensor" and not d.dma:
                    continue
                if d.dma:
                    key, val = ("d", d.dkey), 16 * (d.dma_idx + 1)
                else:
                    key, val = ("p", d.eng), d.seq
                if waits.get(key, 0) < val:
                    waits[key] = val
            if o.bar_waits is not None:
                for dk, cnt in o.bar_waits.items():
                    k = ("d", dk)
                    if waits.get(k, 0) < 16 * cnt:
                        waits[k] = 16 * cnt
            for key, val in sorted(waits.items(), key=str):
                if water.get(key, 0) >= val:
                    continue
                water[key] = val
                sem = dsems[key[1]] if key[0] == "d" else psems[key[1]]
                eng.wait_ge(sem, val)
            ins = o.fn(eng)
            if o.dma:
                ins.then_inc(dsems[o.dkey], 16)
            elif o.inc:
                ins.then_inc(psems[o.eng], 1)




def _finish(p, nc, stack, y, pool, P, D, OT):
    for tt in range(OT):
        p.op("gpsimd", lambda e, tt=tt: e.dma_start(
                out=y[128 * tt:128 * (tt + 1), :], in_=P(D * tt, D)),
             reads=[("x2", D * tt, D * (tt + 1))], writes=[("ydram", 0, 1)], dma=True, tag="ph8_st")
    p.barrier()
    p.finalize()
    psems = {e: stack.enter_context(nc.semaphore("p_" + e)) for e in ENGINES}
    dsems = {dk: stack.enter_context(nc.semaphore(f"d_{dk[0]}_{dk[1]}"))
             for dk in p.dma_count}
    with nc.Block() as block:
        @block.sync
        def _(eng):
            p.emit("sync", eng, psems, dsems)

        @block.tensor
        def _(eng):
            p.emit("tensor", eng, psems, dsems)

        @block.vector
        def _(eng):
            p.emit("vector", eng, psems, dsems)

        @block.scalar
        def _(eng):
            p.emit("scalar", eng, psems, dsems)

        @block.gpsimd
        def _(eng):
            p.emit("gpsimd", eng, psems, dsems)
    nc._stack = stack
    nc._prog = p



class Cfg:
    def __init__(self, D=4096, S=2048, NQ=32, NKV=8):
        self.D, self.S, self.NQ, self.NKV = D, S, NQ, NKV
        self.HD = 128
        self.E, self.R = 8, 16
        assert NQ // NKV == 4
        self.QH = NQ // 2
        self.KH = NKV // 2
        self.OD = (self.QH + 2 * self.KH) * 128
        self.KC = D // 128
        self.TT = S // 128
        self.OWN = S // 2
        self.OT = self.OWN // 128
        self.eps = 1e-5
        self.scale = float(1.0 / np.sqrt(128.0))


def _rope_tables(cfg):
    j = np.arange(64, dtype=np.float64)
    inv = 1.0 / (10000.0 ** (2.0 * j / 128.0))
    t = np.arange(cfg.S, dtype=np.float64)
    ang = np.outer(inv, t)
    C2 = np.concatenate([np.cos(ang)] * 2, axis=0).astype(np.float32)
    S2 = np.concatenate([np.sin(ang)] * 2, axis=0).astype(np.float32)
    return C2, S2


def _chunks(total, step):
    out = []
    o = 0
    while o < total:
        n = min(step, total - o)
        out.append((o, n))
        o += n
    return out



def build_program(cfg, stop_after=99):
    nc = bass.Bass("TRN2", target_bir_lowering=False)
    D, S = cfg.D, cfg.S
    KC, TT, OWN, OT = cfg.KC, cfg.TT, cfg.OWN, cfg.OT
    QH, KH, OD = cfg.QH, cfg.KH, cfg.OD
    OC = OD // 128

    def inp(name, shape):
        return nc.dram_tensor(name, list(shape), F32, kind="ExternalInput").ap()

    xb = inp("xb", (S, D))
    xown = inp("xown", (OWN, D))
    wqkvT = inp("wqkvT", (D, OD))
    woT = inp("woT", (QH * 128, D))
    baseT = inp("baseT", (D, D))
    routT = inp("routT", (D, 8))
    aT = inp("aT", (D, 128))
    bflat = inp("bflat", (128, D))
    ropeC = inp("ropeC", (128, S))
    ropeS = inp("ropeS", (128, S))
    cid = inp("cid", (128, 128))
    cmsk = inp("cmsk", (128, 128))
    crmat = inp("crmat", (128, 128))
    cbc = inp("cbc", (8, 128))

    y = nc.dram_tensor("y", [OWN, D], F32, kind="ExternalOutput").ap()

    xn1s = nc.dram_tensor("xn1s", [D, S], F32, kind="Internal").ap()
    qks = nc.dram_tensor("qks", [(QH + KH) * 128, S], F32, kind="Internal").ap()
    vtms = nc.dram_tensor("vtms", [S, KH * 128], F32, kind="Internal").ap()
    attns = nc.dram_tensor("attns", [QH * 128, S], F32, kind="Internal").ap()
    parts = nc.dram_tensor("parts", [S, D], F32, kind="Internal").ap()
    rsouts = nc.dram_tensor("rsouts", [OWN, D], F32, kind="Internal").ap()
    xn2s = nc.dram_tensor("xn2s", [D, OWN], F32, kind="Internal").ap()

    import contextlib
    stack = contextlib.ExitStack()
    sb = lambda name, shape: stack.enter_context(nc.sbuf_tensor(name, list(shape), F32))

    s_id = sb("s_id", (128, 128))
    s_msk = sb("s_msk", (128, 128))
    s_R = sb("s_R", (128, 128))
    s_bc = sb("s_bc", (8, 128))
    s_C = sb("s_C", (128, S))
    s_S = sb("s_S", (128, S))
    POOLC = max(OT * D, 5 * D, 7 * S, QH * 512)
    pool = sb("pool", (128, POOLC))
    WKC = max(3 * D, 2 * OWN + 1024 + 4 * OWN if False else 0, 2 * OWN + 512 + 4 * OWN, 2 * OWN + 2560 + OWN)
    wk = sb("wk", (128, WKC))
    sm = sb("sm", (128, 64))
    lgsm = sb("lgsm", (128, 8 * OT * 2 + 32))

    psA = stack.enter_context(nc.psum_tensor("psA", [128, 2048], F32))
    psD = stack.enter_context(nc.psum_tensor("psD", [128, 2048], F32))

    p = Prog()

    def P(lo, n):
        return pool[:, lo:lo + n]

    def W(lo, n):
        return wk[:, lo:lo + n]

    for t, src, key in ((s_id, cid, "c_id"), (s_msk, cmsk, "c_msk"),
                        (s_R, crmat, "c_R"), (s_bc, cbc, "c_bc"),
                        (s_C, ropeC, "c_C"), (s_S, ropeS, "c_S")):
        p.op("sync", lambda e, t=t, src=src: e.dma_start(out=t[:], in_=src[:]),
             writes=[(key, 0, 1)], dma=True, tag="const")

    for tt in range(TT):
        sl = tt % 2
        xtm = (f"xtm{sl}", 0, 1)
        p.op("sync", lambda e, tt=tt, sl=sl: e.dma_start(
                out=P(sl * D, D), in_=xb[128 * tt:128 * (tt + 1), :]),
             writes=[xtm], dma=True, tag="ph1_load")
        ms = tt % 2
        p.op("scalar", lambda e, sl=sl, ms=ms: e.activation(
                P(2 * D, D), P(sl * D, D), AF.Square, accum_out=sm[:, 2 + ms:3 + ms]),
             reads=[xtm], writes=[("xsq", 0, 1), ("sx2", ms, ms + 1)], tag="ph1_sq")
        p.op("vector", lambda e, sl=sl, ms=ms: e.reduce_sum(
                sm[:, 0 + ms:1 + ms], P(sl * D, D), axis=AX.X),
             reads=[xtm], writes=[("sx", ms, ms + 1)], tag="ph1_sum")
        p.op("vector", lambda e, ms=ms: e.tensor_scalar(
                sm[:, 4 + ms:5 + ms], sm[:, 0 + ms:1 + ms], 1.0 / D, None, ALU.mult),
             reads=[("sx", ms, ms + 1)], writes=[("mean", ms, ms + 1)], tag="ph1_mean")
        p.op("vector", lambda e, ms=ms: e.tensor_tensor(
                sm[:, 6 + ms:7 + ms], sm[:, 4 + ms:5 + ms], sm[:, 4 + ms:5 + ms], ALU.mult),
             reads=[("mean", ms, ms + 1)], writes=[("msq", ms, ms + 1)], tag="ph1_msq")
        p.op("vector", lambda e, ms=ms: e.tensor_scalar(
                sm[:, 8 + ms:9 + ms], sm[:, 2 + ms:3 + ms], 1.0 / D, None, ALU.mult),
             reads=[("sx2", ms, ms + 1)], writes=[("var0", ms, ms + 1)], tag="ph1_var0")
        p.op("vector", lambda e, ms=ms: e.tensor_scalar(
                sm[:, 10 + ms:11 + ms], sm[:, 8 + ms:9 + ms],
                sm[:, 6 + ms:7 + ms], cfg.eps, ALU.subtract, ALU.add),
             reads=[("var0", ms, ms + 1), ("msq", ms, ms + 1)],
             writes=[("vpe", ms, ms + 1)], tag="ph1_var")
        p.op("scalar", lambda e, ms=ms: e.activation(
                sm[:, 12 + ms:13 + ms], sm[:, 10 + ms:11 + ms], AF.Ln),
             reads=[("vpe", ms, ms + 1)], writes=[("lnv", ms, ms + 1)], tag="ph1_ln")
        p.op("scalar", lambda e, ms=ms: e.activation(
                sm[:, 14 + ms:15 + ms], sm[:, 12 + ms:13 + ms], AF.Exp, scale=-0.5),
             reads=[("lnv", ms, ms + 1)], writes=[("rstd", ms, ms + 1)], tag="ph1_rstd")
        p.op("vector", lambda e, sl=sl, ms=ms: e.tensor_scalar(
                P(3 * D, D), P(sl * D, D), sm[:, 4 + ms:5 + ms],
                sm[:, 14 + ms:15 + ms], ALU.subtract, ALU.mult),
             reads=[xtm, ("mean", ms, ms + 1), ("rstd", ms, ms + 1)],
             writes=[("xnt", 0, 1)], tag="ph1_apply")
        for c in range(KC):
            b8 = c % 8
            trt, trk = (psA, "psA") if b8 < 4 else (psD, "psD")
            tco = 512 * (b8 % 4)
            p.op("tensor", lambda e, c=c, trt=trt, tco=tco: e.transpose(
                    trt[:, tco:tco + 128], P(3 * D + 128 * c, 128), s_id[:]),
                 reads=[("xnt", 0, 1), ("c_id", 0, 1)],
                 writes=[(trk, tco, tco + 128)], tag="ph1_tr")
            p.op("vector", lambda e, c=c, trt=trt, tco=tco: e.tensor_copy(
                    P(4 * D + 128 * c, 128), trt[:, tco:tco + 128]),
                 reads=[(trk, tco, tco + 128)],
                 writes=[("xnfm", 128 * c, 128 * (c + 1))], tag="ph1_ev")
        dstf = xn1s.rearrange("(c q) t -> q c t", q=128)[:, :, 128 * tt:128 * (tt + 1)]
        srcf = P(4 * D, D).rearrange("q (c j) -> q c j", c=KC)
        for c0 in range(0, KC, 8):
            c1 = min(c0 + 8, KC)
            p.op("gpsimd", lambda e, dst=dstf[:, c0:c1, :], src=srcf[:, c0:c1, :]:
                    e.dma_start(out=dst, in_=src),
                 reads=[("xnfm", 0, D)], writes=[("xn1s", 0, 1)], dma=True, tag="ph1_st")

    p.barrier()

    if stop_after < 2:
        _finish(p, nc, stack, y, pool, P, D, OT)
        return nc
    def qkv_store(oc):
        is_q = oc < QH
        is_k = QH <= oc < QH + KH
        psrc, pkey = (psA, "psA") if oc % 2 == 0 else (psD, "psD")
        if is_q or is_k:
            if is_q:
                p.op("vector", lambda e, psrc=psrc: e.tensor_scalar(
                        P(2 * S, S), psrc[:, 0:S], cfg.scale, None, ALU.mult),
                     reads=[(pkey, 0, S)], writes=[("qraw", 0, 1)], tag="ph2_evq")
            else:
                p.op("scalar", lambda e, psrc=psrc: e.copy(P(2 * S, S), psrc[:, 0:S]),
                     reads=[(pkey, 0, S)], writes=[("qraw", 0, 1)], tag="ph2_evk")
            for (t0, n) in _chunks(S, 512):
                p.op("tensor", lambda e, t0=t0, n=n, psrc=psrc: e.matmul(
                        psrc[:, t0:t0 + n], s_R[:], P(2 * S + t0, n),
                        start=True, stop=True),
                     reads=[("qraw", 0, 1), ("c_R", 0, 1)],
                     writes=[(pkey, t0, t0 + n)], tag="ph2_rmm")
            p.op("vector", lambda e: e.tensor_tensor(
                    P(3 * S, S), P(2 * S, S), s_C[:], ALU.mult),
                 reads=[("qraw", 0, 1), ("c_C", 0, 1)], writes=[("q1", 0, 1)], tag="ph2_qc")
            p.op("vector", lambda e, psrc=psrc: e.tensor_tensor(
                    P(4 * S, S), psrc[:, 0:S], s_S[:], ALU.mult),
                 reads=[(pkey, 0, S), ("c_S", 0, 1)], writes=[("q2", 0, 1)], tag="ph2_qs")
            p.op("vector", lambda e: e.tensor_tensor(
                    P(5 * S, S), P(3 * S, S), P(4 * S, S), ALU.add),
                 reads=[("q1", 0, 1), ("q2", 0, 1)], writes=[("qout", 0, 1)], tag="ph2_qo")
            row = 128 * oc
            p.op("gpsimd", lambda e, row=row: e.dma_start(
                    out=qks[row:row + 128, :], in_=P(5 * S, S)),
                 reads=[("qout", 0, 1)], writes=[("qks", 0, 1)], dma=True, tag="ph2_stq")
        else:
            vc = oc - QH - KH
            p.op("scalar", lambda e, psrc=psrc: e.copy(P(2 * S, S), psrc[:, 0:S]),
                 reads=[(pkey, 0, S)], writes=[("qraw", 0, 1)], tag="ph2_evv")
            for tb in range(TT):
                bank = tb % 4
                p.op("tensor", lambda e, tb=tb, bank=bank, psrc=psrc: e.transpose(
                        psrc[:, 512 * bank:512 * bank + 128], P(2 * S + 128 * tb, 128), s_id[:]),
                     reads=[("qraw", 0, 1), ("c_id", 0, 1)],
                     writes=[(pkey, 512 * bank, 512 * bank + 128)], tag="ph2_vtr")
                p.op("vector", lambda e, tb=tb, bank=bank, psrc=psrc: e.tensor_copy(
                        P(3 * S + 128 * tb, 128), psrc[:, 512 * bank:512 * bank + 128]),
                     reads=[(pkey, 512 * bank, 512 * bank + 128)],
                     writes=[("q1", 128 * tb, 128 * (tb + 1))], tag="ph2_vev")
            dstf = vtms.rearrange("(tb q) v -> q tb v", q=128)[:, :, 128 * vc:128 * (vc + 1)]
            srcf = P(3 * S, S).rearrange("q (tb j) -> q tb j", tb=TT)
            for b0 in range(0, TT, 8):
                b1 = min(b0 + 8, TT)
                p.op("gpsimd", lambda e, dst=dstf[:, b0:b1, :], src=srcf[:, b0:b1, :]:
                        e.dma_start(out=dst, in_=src),
                     reads=[("q1", 0, S)], writes=[("vtms", 0, 1)], dma=True, tag="ph2_stv")

    for og in range(OC // 2):
        for k in range(KC):
            wsl = k % 2
            p.op("sync", lambda e, og=og, k=k, wsl=wsl: e.dma_start(
                    out=W(256 * wsl, 256),
                    in_=wqkvT[128 * k:128 * (k + 1), 256 * og:256 * (og + 1)]),
                 writes=[(f"wp{wsl}", 0, 1)], dma=True, tag="ph2_w")
            msl = k % 2
            p.op("scalar", lambda e, k=k, msl=msl: e.dma_start(
                    out=P(msl * S, S), in_=xn1s[128 * k:128 * (k + 1), :]),
                 reads=[("xn1s", 0, 1)], writes=[(f"mv{msl}", 0, 1)], dma=True, tag="ph2_x")
            for oci, ps, pkey in ((0, psA, "psA"), (1, psD, "psD")):
                for (t0, n) in _chunks(S, 512):
                    p.op("tensor", lambda e, k=k, wsl=wsl, msl=msl, oci=oci, t0=t0, n=n, ps=ps: e.matmul(
                            ps[:, t0:t0 + n], W(256 * wsl + 128 * oci, 128),
                            P(msl * S + t0, n), start=(k == 0), stop=(k == KC - 1)),
                         reads=[(f"wp{wsl}", 0, 1), (f"mv{msl}", 0, 1)],
                         writes=[(pkey, t0, t0 + n)], tag="ph2_mm")
        qkv_store(2 * og)
        qkv_store(2 * og + 1)

    p.barrier()

    if stop_after < 3:
        _finish(p, nc, stack, y, pool, P, D, OT)
        return nc
    for j in range(KH):
        p.op("sync", lambda e, j=j: e.dma_start(
                out=P(0, S), in_=qks[128 * (QH + j):128 * (QH + j + 1), :]),
             reads=[("qks", 0, 1)], writes=[("aK", 0, 1)], dma=True, tag="ph3_K")
        vsrc = vtms.rearrange("(tb q) v -> q tb v", q=128)[:, :, 128 * j:128 * (j + 1)]
        vdst = P(S, S).rearrange("q (tb j) -> q tb j", tb=TT)
        for b0 in range(0, TT, 8):
            b1 = min(b0 + 8, TT)
            p.op("sync", lambda e, src=vsrc[:, b0:b1, :], dst=vdst[:, b0:b1, :]:
                    e.dma_start(out=dst, in_=src),
                 reads=[("vtms", 0, 1)], writes=[("aV", 0, 1)], dma=True, tag="ph3_V")
        for qh in range(4):
            hq = 4 * j + qh
            p.op("sync", lambda e, hq=hq: e.dma_start(
                    out=P(2 * S, S), in_=qks[128 * hq:128 * (hq + 1), :]),
                 reads=[("qks", 0, 1)], writes=[("aQ", 0, 1)], dma=True, tag="ph3_Q")
            for g in range(TT):
                klen = 128 * (g + 1)
                for kb in range(0, g + 1, 4):
                    n = min(512, klen - 128 * kb)
                    p.op("tensor", lambda e, g=g, kb=kb, n=n: e.matmul(
                            psA[:, 128 * kb:128 * kb + n], P(2 * S + 128 * g, 128),
                            P(128 * kb, n), start=True, stop=True),
                         reads=[("aQ", 0, 1), ("aK", 0, 1)],
                         writes=[("psA", 128 * kb, 128 * kb + n)], tag="ph3_s")
                p.op("vector", lambda e, g=g: e.tensor_tensor(
                        psD[:, 0:128], psA[:, 128 * g:128 * (g + 1)], s_msk[:], ALU.add),
                     reads=[("psA", 128 * g, 128 * (g + 1)), ("c_msk", 0, 1)],
                     writes=[("psD", 0, 128)], tag="ph3_msk")
                if g == 0:
                    p.op("vector", lambda e: e.reduce_max(
                            sm[:, 16:17], psD[:, 0:128], axis=AX.X, negate=True),
                         reads=[("psD", 0, 128)], writes=[("negmx", 0, 1)], tag="ph3_mx")
                else:
                    p.op("vector", lambda e, g=g: e.reduce_max(
                            sm[:, 18:19], psA[:, 0:128 * g], axis=AX.X),
                         reads=[("psA", 0, 128 * g)], writes=[("rm", 0, 1)], tag="ph3_m1")
                    p.op("vector", lambda e: e.reduce_max(
                            sm[:, 19:20], psD[:, 0:128], axis=AX.X),
                         reads=[("psD", 0, 128)], writes=[("rm", 1, 2)], tag="ph3_m2")
                    p.op("vector", lambda e: e.reduce_max(
                            sm[:, 16:17], sm[:, 18:20], axis=AX.X, negate=True),
                         reads=[("rm", 0, 2)], writes=[("negmx", 0, 1)], tag="ph3_mx")
                if g > 0:
                    p.op("scalar", lambda e, g=g: e.activation(
                            P(3 * S, 128 * g), psA[:, 0:128 * g], AF.Exp,
                            bias=sm[:, 16:17], scale=1.0, accum_out=sm[:, 20:21]),
                         reads=[("psA", 0, 128 * g), ("negmx", 0, 1)],
                         writes=[("aP", 0, 128 * g), ("s1", 0, 1)], tag="ph3_exp")
                p.op("scalar", lambda e, g=g: e.activation(
                        P(3 * S + 128 * g, 128), psD[:, 0:128], AF.Exp,
                        bias=sm[:, 16:17], scale=1.0, accum_out=sm[:, 21:22]),
                     reads=[("psD", 0, 128), ("negmx", 0, 1)],
                     writes=[("aP", 128 * g, 128 * (g + 1)), ("s2", 0, 1)], tag="ph3_expd")
                if g > 0:
                    p.op("vector", lambda e: e.tensor_tensor(
                            sm[:, 22:23], sm[:, 20:21], sm[:, 21:22], ALU.add),
                         reads=[("s1", 0, 1), ("s2", 0, 1)], writes=[("ssum", 0, 1)], tag="ph3_sm")
                else:
                    p.op("vector", lambda e: e.tensor_copy(sm[:, 22:23], sm[:, 21:22]),
                         reads=[("s2", 0, 1)], writes=[("ssum", 0, 1)], tag="ph3_sm")
                p.op("vector", lambda e: e.reciprocal(sm[:, 23:24], sm[:, 22:23]),
                     reads=[("ssum", 0, 1)], writes=[("rinv", 0, 1)], tag="ph3_ri")
                p.op("vector", lambda e, klen=klen: e.tensor_scalar(
                        P(4 * S, klen), P(3 * S, klen), sm[:, 23:24], None, ALU.mult),
                     reads=[("aP", 0, klen), ("rinv", 0, 1)],
                     writes=[("aPn", 0, klen)], tag="ph3_n")
                for kb in range(g + 1):
                    tco = 512 if kb % 2 == 0 else 1536
                    p.op("tensor", lambda e, kb=kb, tco=tco: e.transpose(
                            psD[:, tco:tco + 128],
                            P(4 * S + 128 * kb, 128), s_id[:]),
                         reads=[("aPn", 128 * kb, 128 * (kb + 1)), ("c_id", 0, 1)],
                         writes=[("psD", tco, tco + 128)], tag="ph3_tr")
                    if kb % 2 == 0:
                        p.op("vector", lambda e, kb=kb, tco=tco: e.tensor_copy(
                                P(5 * S + 128 * kb, 128), psD[:, tco:tco + 128]),
                             reads=[("psD", tco, tco + 128)],
                             writes=[("aPT", 128 * kb, 128 * (kb + 1))], tag="ph3_trev")
                    else:
                        p.op("scalar", lambda e, kb=kb, tco=tco: e.copy(
                                P(5 * S + 128 * kb, 128), psD[:, tco:tco + 128]),
                             reads=[("psD", tco, tco + 128)],
                             writes=[("aPT", 128 * kb, 128 * (kb + 1))], tag="ph3_trev")
                for kb in range(g + 1):
                    p.op("tensor", lambda e, kb=kb, g=g: e.matmul(
                            psD[:, 1024:1152], P(S + 128 * kb, 128),
                            P(5 * S + 128 * kb, 128), start=(kb == 0), stop=(kb == g)),
                         reads=[("aV", 0, 1), ("aPT", 128 * kb, 128 * (kb + 1))],
                         writes=[("psD", 1024, 1152)], tag="ph3_pv")
                p.op("vector", lambda e, g=g: e.tensor_copy(
                        P(6 * S + 128 * g, 128), psD[:, 1024:1152]),
                     reads=[("psD", 1024, 1152)],
                     writes=[("aao", 128 * g, 128 * (g + 1))], tag="ph3_aev")
            p.op("gpsimd", lambda e, hq=hq: e.dma_start(
                    out=attns[128 * hq:128 * (hq + 1), :], in_=P(6 * S, S)),
                 reads=[("aao", 0, S)], writes=[("attns", 0, 1)], dma=True, tag="ph3_st")

    p.barrier()

    if stop_after < 4:
        _finish(p, nc, stack, y, pool, P, D, OT)
        return nc
    for (tq0, tqn) in _chunks(S, 512):
        asrc = attns.rearrange("(kc q) t -> q kc t", q=128)[:, :, tq0:tq0 + tqn]
        adst = P(0, QH * tqn).rearrange("q (kc j) -> q kc j", kc=QH)
        for b0 in range(0, QH, 8):
            b1 = min(b0 + 8, QH)
            p.op("sync", lambda e, src=asrc[:, b0:b1, :], dst=adst[:, b0:b1, :]:
                    e.dma_start(out=dst, in_=src),
                 reads=[("attns", 0, 1)], writes=[("wat", 0, 1)], dma=True, tag="ph4_a")
        for og in range(D // 512):
            par = og % 2
            ps, pkey = (psA, "psA") if par == 0 else (psD, "psD")
            for k in range(QH):
                wsl = k % 2
                p.op("sync", lambda e, k=k, og=og, wsl=wsl: e.dma_start(
                        out=W(512 * wsl, 512),
                        in_=woT[128 * k:128 * (k + 1), 512 * og:512 * (og + 1)]),
                     writes=[(f"wo{wsl}", 0, 1)], dma=True, tag="ph4_w")
                for tt in range(tqn // 128):
                    p.op("tensor", lambda e, k=k, tt=tt, wsl=wsl, ps=ps, tqn=tqn: e.matmul(
                            ps[:, 512 * tt:512 * (tt + 1)], P(tqn * k + 128 * tt, 128),
                            W(512 * wsl, 512), start=(k == 0), stop=(k == QH - 1)),
                         reads=[("wat", 0, 1), (f"wo{wsl}", 0, 1)],
                         writes=[(pkey, 512 * tt, 512 * (tt + 1))], tag="ph4_mm")
            for tt in range(tqn // 128):
                psl = tt % 2
                p.op("vector", lambda e, tt=tt, psl=psl, ps=ps: e.tensor_copy(
                        W(1024 + 512 * psl, 512), ps[:, 512 * tt:512 * (tt + 1)]),
                     reads=[(pkey, 512 * tt, 512 * (tt + 1))],
                     writes=[(f"po{psl}", 0, 1)], tag="ph4_ev")
                p.op("gpsimd", lambda e, tq0=tq0, og=og, tt=tt, psl=psl: e.dma_start(
                        out=parts[tq0 + 128 * tt:tq0 + 128 * (tt + 1),
                                  512 * og:512 * (og + 1)],
                        in_=W(1024 + 512 * psl, 512)),
                     reads=[(f"po{psl}", 0, 1)], writes=[("parts", 0, 1)],
                     dma=True, tag="ph4_st")

    p.barrier()

    if stop_after < 5:
        _finish(p, nc, stack, y, pool, P, D, OT)
        return nc
    p.op("gpsimd", lambda e: e.collective_compute(
            "ReduceScatter", ALU.add,
            replica_groups=[[0, 1], [2, 3], [4, 5], [6, 7]],
            ins=[parts[:]], outs=[rsouts[:]]),
         reads=[("parts", 0, 1)], writes=[("rsouts", 0, 1)], tag="ph5_rs")

    p.barrier()

    if stop_after < 6:
        _finish(p, nc, stack, y, pool, P, D, OT)
        return nc
    for tt in range(OT):
        p.op("sync", lambda e, tt=tt: e.dma_start(
                out=W(0, D), in_=rsouts[128 * tt:128 * (tt + 1), :]),
             reads=[("rsouts", 0, 1)], writes=[("rs6", 0, 1)], dma=True, tag="ph6_rs")
        p.op("sync", lambda e, tt=tt: e.dma_start(
                out=W(D, D), in_=xown[128 * tt:128 * (tt + 1), :]),
             writes=[("xo6", 0, 1)], dma=True, tag="ph6_xo")
        x2k = ("x2", D * tt, D * (tt + 1))
        p.op("vector", lambda e, tt=tt: e.tensor_tensor(
                P(D * tt, D), W(0, D), W(D, D), ALU.add),
             reads=[("rs6", 0, 1), ("xo6", 0, 1)], writes=[x2k], tag="ph6_add")
        p.op("scalar", lambda e, tt=tt: e.activation(
                W(0, D), P(D * tt, D), AF.Square, accum_out=sm[:, 26:27]),
             reads=[x2k], writes=[("rs6", 0, 1), ("sx26", 0, 1)], tag="ph6_sq")
        p.op("vector", lambda e, tt=tt: e.reduce_sum(
                sm[:, 24:25], P(D * tt, D), axis=AX.X),
             reads=[x2k], writes=[("sx6", 0, 1)], tag="ph6_sum")
        p.op("vector", lambda e: e.tensor_scalar(
                sm[:, 28:29], sm[:, 24:25], 1.0 / D, None, ALU.mult),
             reads=[("sx6", 0, 1)], writes=[("mean6", 0, 1)], tag="ph6_mean")
        p.op("vector", lambda e: e.tensor_tensor(
                sm[:, 29:30], sm[:, 28:29], sm[:, 28:29], ALU.mult),
             reads=[("mean6", 0, 1)], writes=[("msq6", 0, 1)], tag="ph6_msq")
        p.op("vector", lambda e: e.tensor_scalar(
                sm[:, 30:31], sm[:, 26:27], 1.0 / D, None, ALU.mult),
             reads=[("sx26", 0, 1)], writes=[("var6", 0, 1)], tag="ph6_var0")
        p.op("vector", lambda e: e.tensor_scalar(
                sm[:, 31:32], sm[:, 30:31], sm[:, 29:30], cfg.eps, ALU.subtract, ALU.add),
             reads=[("var6", 0, 1), ("msq6", 0, 1)], writes=[("vpe6", 0, 1)], tag="ph6_var")
        p.op("scalar", lambda e: e.activation(sm[:, 32:33], sm[:, 31:32], AF.Ln),
             reads=[("vpe6", 0, 1)], writes=[("lnv6", 0, 1)], tag="ph6_ln")
        p.op("scalar", lambda e: e.activation(sm[:, 33:34], sm[:, 32:33], AF.Exp, scale=-0.5),
             reads=[("lnv6", 0, 1)], writes=[("rstd6", 0, 1)], tag="ph6_rstd")
        p.op("vector", lambda e, tt=tt: e.tensor_scalar(
                W(2 * D, D), P(D * tt, D), sm[:, 28:29], sm[:, 33:34],
                ALU.subtract, ALU.mult),
             reads=[x2k, ("mean6", 0, 1), ("rstd6", 0, 1)],
             writes=[("xnt6", 0, 1)], tag="ph6_apply")
        for c in range(KC):
            b8 = c % 8
            trt, trk = (psA, "psA") if b8 < 4 else (psD, "psD")
            tco = 512 * (b8 % 4)
            p.op("tensor", lambda e, c=c, trt=trt, tco=tco: e.transpose(
                    trt[:, tco:tco + 128], W(2 * D + 128 * c, 128), s_id[:]),
                 reads=[("xnt6", 0, 1), ("c_id", 0, 1)],
                 writes=[(trk, tco, tco + 128)], tag="ph6_tr")
            p.op("vector", lambda e, c=c, trt=trt, tco=tco: e.tensor_copy(
                    W(128 * c, 128), trt[:, tco:tco + 128]),
                 reads=[(trk, tco, tco + 128)],
                 writes=[("rs6", 0, 1)], tag="ph6_ev")
        dstf = xn2s.rearrange("(c q) t -> q c t", q=128)[:, :, 128 * tt:128 * (tt + 1)]
        srcf = W(0, D).rearrange("q (c j) -> q c j", c=KC)
        for c0 in range(0, KC, 8):
            c1 = min(c0 + 8, KC)
            p.op("gpsimd", lambda e, dst=dstf[:, c0:c1, :], src=srcf[:, c0:c1, :]:
                    e.dma_start(out=dst, in_=src),
                 reads=[("rs6", 0, 1)], writes=[("xn2s", 0, 1)], dma=True, tag="ph6_st")

    p.barrier()

    if stop_after < 7:
        _finish(p, nc, stack, y, pool, P, D, OT)
        return nc
    NW = OWN
    A0 = 2 * NW
    LG = 2 * NW + 512
    WFM = LG + NW
    WBIG = WFM + NW
    MIDS = WBIG + NW
    for k in range(KC):
        msl = k % 2
        p.op("scalar", lambda e, k=k, msl=msl: e.dma_start(
                out=W(msl * NW, NW), in_=xn2s[128 * k:128 * (k + 1), :]),
             reads=[("xn2s", 0, 1)], writes=[(f"fm{msl}", 0, 1)], dma=True, tag="ph7_x")
        p.op("sync", lambda e, k=k, msl=msl: e.dma_start(
                out=W(A0 + 136 * msl, 128), in_=aT[128 * k:128 * (k + 1), :]),
             writes=[(f"at{msl}", 0, 1)], dma=True, tag="ph7_a")
        p.op("sync", lambda e, k=k, msl=msl: e.dma_start(
                out=W(A0 + 136 * msl + 128, 8), in_=routT[128 * k:128 * (k + 1), :]),
             writes=[(f"rt{msl}", 0, 1)], dma=True, tag="ph7_r")
        for (t0, n) in _chunks(NW, 512):
            p.op("tensor", lambda e, k=k, msl=msl, t0=t0, n=n: e.matmul(
                    psA[:, t0:t0 + n], W(A0 + 136 * msl, 128),
                    W(msl * NW + t0, n), start=(k == 0), stop=(k == KC - 1)),
                 reads=[(f"at{msl}", 0, 1), (f"fm{msl}", 0, 1)],
                 writes=[("psA", t0, t0 + n)], tag="ph7_mmA")
            p.op("tensor", lambda e, k=k, msl=msl, t0=t0, n=n: e.matmul(
                    psA[0:8, 1024 + t0:1024 + t0 + n], W(A0 + 136 * msl + 128, 8),
                    W(msl * NW + t0, n), start=(k == 0), stop=(k == KC - 1)),
                 reads=[(f"rt{msl}", 0, 1), (f"fm{msl}", 0, 1)],
                 writes=[("psA", 1024 + t0, 1024 + t0 + n)], tag="ph7_mmR")

    p.op("vector", lambda e: e.tensor_copy(W(LG, NW)[0:8, :], psA[0:8, 1024:1024 + NW]),
         reads=[("psA", 1024, 1024 + NW)], writes=[("lg", 0, 1)], tag="ph7_lgev")
    for tt in range(OT):
        p.op("tensor", lambda e, tt=tt: e.transpose(
                psD[:, 8 * tt:8 * (tt + 1)], W(LG, NW)[0:8, 128 * tt:128 * (tt + 1)],
                s_id[0:8, 0:8]),
             reads=[("lg", 0, 1), ("c_id", 0, 1)],
             writes=[("psD", 8 * tt, 8 * (tt + 1))], tag="ph7_lgtr")
    p.op("vector", lambda e: e.tensor_copy(lgsm[:, 0:8 * OT], psD[:, 0:8 * OT]),
         reads=[("psD", 0, 8 * OT)], writes=[("lgt", 0, 8 * OT)], tag="ph7_lgtev")
    T1 = 8 * OT
    T2 = 8 * OT + 8
    WD = 8 * OT + 16
    for tt in range(OT):
        lgt = lgsm[:, 8 * tt:8 * (tt + 1)]
        lk = ("lgt", 8 * tt, 8 * (tt + 1))
        p.op("vector", lambda e, lgt=lgt: e.reduce_max(sm[:, 34:35], lgt, axis=AX.X),
             reads=[lk], writes=[("m1", 0, 1)], tag="ph7_m1")
        p.op("vector", lambda e, lgt=lgt: e.tensor_scalar(
                lgsm[:, T1:T1 + 8], lgt, sm[:, 34:35], -1e30, ALU.is_equal, ALU.mult),
             reads=[lk, ("m1", 0, 1)], writes=[("t1", 0, 1)], tag="ph7_t1")
        p.op("vector", lambda e, lgt=lgt: e.tensor_tensor(
                lgsm[:, T2:T2 + 8], lgt, lgsm[:, T1:T1 + 8], ALU.add),
             reads=[lk, ("t1", 0, 1)], writes=[("t2", 0, 1)], tag="ph7_t2")
        p.op("vector", lambda e: e.reduce_max(sm[:, 35:36], lgsm[:, T2:T2 + 8], axis=AX.X),
             reads=[("t2", 0, 1)], writes=[("m2", 0, 1)], tag="ph7_m2")
        p.op("vector", lambda e: e.tensor_tensor(
                sm[:, 36:37], sm[:, 35:36], sm[:, 34:35], ALU.subtract),
             reads=[("m1", 0, 1), ("m2", 0, 1)], writes=[("dd", 0, 1)], tag="ph7_d")
        p.op("scalar", lambda e: e.activation(sm[:, 37:38], sm[:, 36:37], AF.Exp),
             reads=[("dd", 0, 1)], writes=[("e01", 0, 1)], tag="ph7_e")
        p.op("vector", lambda e: e.tensor_scalar(sm[:, 38:39], sm[:, 37:38], 1.0, None, ALU.add),
             reads=[("e01", 0, 1)], writes=[("t3", 0, 1)], tag="ph7_t3")
        p.op("vector", lambda e: e.reciprocal(sm[:, 39:40], sm[:, 38:39]),
             reads=[("t3", 0, 1)], writes=[("w0r", 0, 1)], tag="ph7_w0r")
        p.op("vector", lambda e: e.tensor_scalar(sm[:, 40:41], sm[:, 39:40], 2.0, None, ALU.mult),
             reads=[("w0r", 0, 1)], writes=[("w0", 0, 1)], tag="ph7_w0")
        p.op("vector", lambda e: e.tensor_scalar(sm[:, 41:42], sm[:, 40:41], -1.0, 2.0, ALU.mult, ALU.add),
             reads=[("w0", 0, 1)], writes=[("w1", 0, 1)], tag="ph7_w1")
        p.op("vector", lambda e, lgt=lgt: e.tensor_scalar(
                lgsm[:, T1:T1 + 8], lgt, sm[:, 34:35], sm[:, 40:41], ALU.is_equal, ALU.mult),
             reads=[lk, ("m1", 0, 1), ("w0", 0, 1)], writes=[("t1", 0, 1)], tag="ph7_c0")
        p.op("vector", lambda e, lgt=lgt: e.tensor_scalar(
                lgsm[:, T2:T2 + 8], lgt, sm[:, 35:36], sm[:, 41:42], ALU.is_equal, ALU.mult),
             reads=[lk, ("m2", 0, 1), ("w1", 0, 1)], writes=[("t2", 0, 1)], tag="ph7_c1")
        p.op("vector", lambda e, tt=tt: e.tensor_tensor(
                lgsm[:, WD + 8 * tt:WD + 8 * (tt + 1)],
                lgsm[:, T1:T1 + 8], lgsm[:, T2:T2 + 8], ALU.add),
             reads=[("t1", 0, 1), ("t2", 0, 1)],
             writes=[("wd", 8 * tt, 8 * (tt + 1))], tag="ph7_wd")
        p.op("tensor", lambda e, tt=tt: e.transpose(
                psD[0:8, 1024 + 128 * tt:1024 + 128 * (tt + 1)],
                lgsm[:, WD + 8 * tt:WD + 8 * (tt + 1)], s_id[:]),
             reads=[("wd", 8 * tt, 8 * (tt + 1)), ("c_id", 0, 1)],
             writes=[("psD", 1024 + 128 * tt, 1024 + 128 * (tt + 1))], tag="ph7_wtr")
    p.op("vector", lambda e: e.tensor_copy(W(WFM, NW)[0:8, :], psD[0:8, 1024:1024 + NW]),
         reads=[("psD", 1024, 1024 + NW)], writes=[("wfm", 0, 1)], tag="ph7_wfev")
    for (t0, n) in _chunks(NW, 512):
        p.op("tensor", lambda e, t0=t0, n=n: e.matmul(
                psD[:, t0:t0 + n], s_bc[:], W(WFM, NW)[0:8, t0:t0 + n],
                start=True, stop=True),
             reads=[("wfm", 0, 1), ("c_bc", 0, 1)],
             writes=[("psD", t0, t0 + n)], tag="ph7_bc")
    p.op("vector", lambda e: e.tensor_copy(W(WBIG, NW), psD[:, 0:NW]),
         reads=[("psD", 0, NW)], writes=[("wbig", 0, 1)], tag="ph7_bigev")
    p.op("vector", lambda e: e.tensor_tensor(
            W(MIDS, NW), psA[:, 0:NW], W(WBIG, NW), ALU.mult),
         reads=[("psA", 0, NW), ("wbig", 0, 1)], writes=[("mids", 0, 1)], tag="ph7_mids")

    p.barrier()

    assert MIDS >= 2 * NW + 1536 + NW or NW < 1024
    PS3 = [(psA, "psA", 0), (psA, "psA", 1024), (psD, "psD", 0)]
    STG = 2 * NW + 1536
    for (o0, nch) in _chunks(KC, 3):
        for k in range(KC):
            msl = k % 2
            p.op("scalar", lambda e, k=k, msl=msl: e.dma_start(
                    out=W(msl * NW, NW), in_=xn2s[128 * k:128 * (k + 1), :]),
                 reads=[("xn2s", 0, 1)], writes=[(f"fm{msl}", 0, 1)], dma=True, tag="ph7b_x")
            p.op("sync", lambda e, k=k, o0=o0, nch=nch, msl=msl: e.dma_start(
                    out=W(2 * NW + 512 * msl, 128 * nch),
                    in_=baseT[128 * k:128 * (k + 1), 128 * o0:128 * (o0 + nch)]),
                 writes=[(f"wb{msl}", 0, 1)], dma=True, tag="ph7b_w")
            for ci in range(nch):
                ps, pkey, poff = PS3[ci]
                for (t0, n) in _chunks(NW, 512):
                    p.op("tensor", lambda e, k=k, msl=msl, ci=ci, t0=t0, n=n, ps=ps, poff=poff: e.matmul(
                            ps[:, poff + t0:poff + t0 + n],
                            W(2 * NW + 512 * msl + 128 * ci, 128),
                            W(msl * NW + t0, n), start=(k == 0), stop=False),
                         reads=[(f"wb{msl}", 0, 1), (f"fm{msl}", 0, 1)],
                         writes=[(pkey, poff + t0, poff + t0 + n)], tag="ph7b_mm")
        p.op("sync", lambda e, o0=o0, nch=nch: e.dma_start(
                out=W(2 * NW + 1024, 128 * nch),
                in_=bflat[:, 128 * o0:128 * (o0 + nch)]),
             writes=[("wlb", 0, 1)], dma=True, tag="ph7b_lb")
        for ci in range(nch):
            ps, pkey, poff = PS3[ci]
            for (t0, n) in _chunks(NW, 512):
                p.op("tensor", lambda e, ci=ci, t0=t0, n=n, ps=ps, poff=poff: e.matmul(
                        ps[:, poff + t0:poff + t0 + n],
                        W(2 * NW + 1024 + 128 * ci, 128),
                        W(MIDS + t0, n), start=False, stop=True),
                     reads=[("wlb", 0, 1), ("mids", 0, 1)],
                     writes=[(pkey, poff + t0, poff + t0 + n)], tag="ph7b_mlb")
        for ci in range(nch):
            ps, pkey, poff = PS3[ci]
            oc = o0 + ci
            p.op("vector", lambda e, ps=ps, poff=poff: e.tensor_copy(
                    W(STG, NW), ps[:, poff:poff + NW]),
                 reads=[(pkey, poff, poff + NW)], writes=[("stg", 0, 1)], tag="ph7b_ev")
            for tt in range(OT):
                tco = 1024 if tt % 2 == 0 else 1536
                p.op("tensor", lambda e, tt=tt, tco=tco: e.transpose(
                        psD[:, tco:tco + 128],
                        W(STG + 128 * tt, 128), s_id[:]),
                     reads=[("stg", 0, 1), ("c_id", 0, 1)],
                     writes=[("psD", tco, tco + 128)], tag="ph7b_tr")
                p.op("vector", lambda e, tt=tt, tco=tco, oc=oc: e.tensor_tensor(
                        P(D * tt + 128 * oc, 128), P(D * tt + 128 * oc, 128),
                        psD[:, tco:tco + 128], ALU.add),
                     reads=[("psD", tco, tco + 128),
                            ("x2", D * tt + 128 * oc, D * tt + 128 * (oc + 1))],
                     writes=[("x2", D * tt + 128 * oc, D * tt + 128 * (oc + 1))], tag="ph7b_add")

    _finish(p, nc, stack, y, pool, P, D, OT)
    return nc



def make_in_maps(cfg, x, Wqkv, Wo, base_W, router_W, lora_A, lora_B):
    D, S, NQ, NKV = cfg.D, cfg.S, cfg.NQ, cfg.NKV
    QH, KH = cfg.QH, cfg.KH
    C2, S2 = _rope_tables(cfg)
    ident = np.eye(128, dtype=np.float32)
    msk = np.where(np.arange(128)[None, :] > np.arange(128)[:, None],
                   np.float32(-1e30), np.float32(0.0)).astype(np.float32)
    rmat = np.zeros((128, 128), dtype=np.float32)
    for m in range(64):
        rmat[m + 64, m] = -1.0
        rmat[m, m + 64] = 1.0
    bc = np.zeros((8, 128), dtype=np.float32)
    for e in range(8):
        bc[e, 16 * e:16 * (e + 1)] = 1.0

    WqkvT = np.ascontiguousarray(Wqkv.T)
    WoT = np.ascontiguousarray(Wo.T)
    baseT = np.ascontiguousarray(base_W.T)
    routT = np.ascontiguousarray(router_W.T)
    aT = np.ascontiguousarray(lora_A.transpose(1, 0, 2).reshape(D, 128))
    bflat = np.ascontiguousarray(lora_B.reshape(128, D))

    qd, kd = NQ * 128, NKV * 128
    in_maps = []
    for c in range(8):
        b, h = c // 2, c % 2
        wq = WqkvT[:, 128 * QH * h:128 * QH * (h + 1)]
        wk_ = WqkvT[:, qd + 128 * KH * h: qd + 128 * KH * (h + 1)]
        wv = WqkvT[:, qd + kd + 128 * KH * h: qd + kd + 128 * KH * (h + 1)]
        m = {
            "xb": np.ascontiguousarray(x[b]),
            "xown": np.ascontiguousarray(x[b, (S // 2) * h:(S // 2) * (h + 1)]),
            "wqkvT": np.ascontiguousarray(np.concatenate([wq, wk_, wv], axis=1)),
            "woT": np.ascontiguousarray(WoT[128 * QH * h:128 * QH * (h + 1), :]),
            "baseT": baseT, "routT": routT, "aT": aT, "bflat": bflat,
            "ropeC": C2, "ropeS": S2, "cid": ident, "cmsk": msk,
            "crmat": rmat, "cbc": bc,
        }
        in_maps.append(m)
    return in_maps


def kernel(L0_attn_hidden_states, L0_attn_Wqkv, L0_attn_Wo,
           L0_ffn_base_W, L0_ffn_router_W, L0_ffn_lora_A, L0_ffn_lora_B):
    cfg = Cfg()
    x = np.ascontiguousarray(np.asarray(L0_attn_hidden_states, dtype=np.float32))
    in_maps = make_in_maps(cfg, x,
                           np.asarray(L0_attn_Wqkv, dtype=np.float32),
                           np.asarray(L0_attn_Wo, dtype=np.float32),
                           np.asarray(L0_ffn_base_W, dtype=np.float32),
                           np.asarray(L0_ffn_router_W, dtype=np.float32),
                           np.asarray(L0_ffn_lora_A, dtype=np.float32),
                           np.asarray(L0_ffn_lora_B, dtype=np.float32))
    nc = build_program(cfg)
    res = run_bass_kernel_spmd(nc, in_maps, core_ids=list(range(8)))
    out = np.empty((4, cfg.S, cfg.D), dtype=np.float32)
    half = cfg.S // 2
    for c in range(8):
        b, h = c // 2, c % 2
        out[b, half * h:half * (h + 1)] = res.results[c]["y"]
    return out
